# version 29
# baseline (speedup 1.0000x reference)
import contextlib
import numpy as np
import ml_dtypes

import concourse.bass as bass
import concourse.mybir as mybir
from concourse.bass_utils import run_bass_kernel_spmd

F32 = mybir.dt.float32
BF16 = mybir.dt.bfloat16
I32 = mybir.dt.int32
ALU = mybir.AluOpType
AF = mybir.ActivationFunctionType

NCORE = 8
B, S, E, H, Dh = 2, 8192, 1024, 16, 64
T = 2048
NT = 16
NG = 8
M32 = float(np.float32(1.5 * 2 ** 23))
M64 = float(1.5 * 2 ** 52)
EPS_DEN = 1e-6
EPS_LN = 1e-5

_DEBUG_TAPS = ()


_CHAINED_ENGINES = {"gpsimd", "vector", "scalar"}


class _Rec:
    ENGINES = ("sync", "gpsimd", "vector", "scalar", "tensor")

    def __init__(self):
        self.ops = {e: [] for e in self.ENGINES}
        self.cnt = {}

    def new_sem(self, name):
        self.cnt[name] = 0

    CHAIN = {"vector": "VE", "scalar": "AC", "gpsimd": "GD"}

    def emit(self, eng, fn, waits=(), inc=None, dma=False):
        waits = list(waits)
        ch = self.CHAIN.get(eng)
        if ch is not None and not dma and fn is not None:
            if eng in _CHAINED_ENGINES and self.cnt[ch] > 0:
                waits.append((ch, self.cnt[ch]))
            if inc is None:
                inc = (ch, 1)
            elif inc[0] != ch:
                raise ValueError(f"op on {eng} must inc {ch}, got {inc}")
        if inc is not None:
            self.cnt[inc[0]] += inc[1]
        self.ops[eng].append((fn, tuple(waits), inc))

    def wait_all(self, eng, *sems):
        w = tuple((s, self.cnt[s]) for s in sems if self.cnt[s] > 0)
        if w:
            self.emit(eng, None, waits=w)

    def replay(self, eng_name, e, sems):
        for fn, waits, inc in self.ops[eng_name]:
            for (s, v) in waits:
                e.wait_ge(sems[s], v)
            if fn is None:
                continue
            ins = fn(e, sems)
            if inc is not None:
                ins.then_inc(sems[inc[0]], inc[1])


def build_nc(debug_taps=(), detect_races=False):
    nc = bass.Bass("TRN2", target_bir_lowering=False,
                   detect_race_conditions=detect_races)

    x_h = nc.dram_tensor("x", [T, E], F32, kind="ExternalInput").ap()
    wqb_h = nc.dram_tensor("wqb", [128, 1024], F32, kind="ExternalInput").ap()
    wkvb_h = nc.dram_tensor("wkvb", [128, 2048], F32, kind="ExternalInput").ap()
    wo_h = nc.dram_tensor("wo", [E, E], F32, kind="ExternalInput").ap()
    wt_h = nc.dram_tensor("wt", [E, E], I32, kind="ExternalInput").ap()
    bias_h = nc.dram_tensor("bias", [1, E], F32, kind="ExternalInput").ap()
    scol_h = nc.dram_tensor("scol", [128, 1], F32, kind="ExternalInput").ap()
    ident_h = nc.dram_tensor("ident", [128, 128], BF16, kind="ExternalInput").ap()
    selc_h = nc.dram_tensor("selc", [16, 1024], BF16, kind="ExternalInput").ap()
    onesr_h = nc.dram_tensor("onesr", [1, 128], BF16, kind="ExternalInput").ap()
    out_h = nc.dram_tensor("out", [T, E], F32, kind="ExternalOutput").ap()
    cc_in = nc.dram_tensor("cc_in", [128, 520], F32)
    cc_out = nc.dram_tensor("cc_out", [128, 520], F32)

    dbg = {}
    for name, shape, dt in (
        ("dbg_xfm", [128, 16384], BF16), ("dbg_kf", [128, 16384], BF16),
        ("dbg_v", [128, 16640], BF16), ("dbg_kvz", [128, 520], F32),
        ("dbg_qf", [128, 16384], BF16), ("dbg_ctx", [128, 16384], BF16),
        ("dbg_den", [16, 2048], F32), ("dbg_t", [128, 16384], F32),
        ("dbg_xint", [128, 16384], BF16), ("dbg_h2", [128, 2048], F32),
        ("dbg_kvzl", [128, 520], F32), ("dbg_stats", [128, 160], F32),
        ("dbg_h", [128, 16384], F32),
        ("dbg_xintfm", [128, 16384], BF16), ("dbg_wtbf", [128, 8192], BF16),
    ):
        if name in debug_taps:
            dbg[name] = nc.dram_tensor(name, shape, dt, kind="ExternalOutput").ap()

    A = nc.alloc_sbuf_tensor_at
    off = {}
    SB0 = 16512

    def at(name, shape, dt, o):
        off[name] = o
        return A(name, shape, dt, offset=SB0 + o)

    S1 = 0
    S2 = 33280
    S3 = 66560
    S4 = 99840

    xbf = at("xbf", [128, 16384], BF16, S1)
    qf = at("qf", [128, 16384], BF16, S1)
    xint_tm = at("xint_tm", [128, 16384], BF16, S1)
    outstage = at("outstage", [128, 8192], F32, S1)
    kf = at("kf", [128, 16384], BF16, S2)
    ctx = at("ctx", [128, 16384], BF16, S2)
    xint_fm = at("xint_fm", [128, 16384], BF16, S2)
    vsb = at("vsb", [128, 16640], BF16, S3)
    t_lo = at("t_lo", [128, 8192], F32, S3)
    xfm = at("xfm", [128, 16384], BF16, S4)
    den = at("den", [16, 2048], F32, S4)
    rd = at("rd", [16, 2048], F32, S4 + 8192)
    rdb16 = at("rdb16", [16, 2048], BF16, S4 + 16384)
    numbf = at("numbf", [128, 1024], BF16, S4 + 20480)
    rdbe = at("rdbe", [128, 1024], BF16, S4 + 22528)
    t_hi = at("t_hi", [128, 8192], F32, S4)

    wo_bf = at("wo_bf", [128, 8192], BF16, 132608)
    wt_bf = at("wt_bf", [128, 8192], BF16, 148992)
    xstream = at("xstream", [128, 2048], F32, 165376)
    tmp1 = at("tmp1", [128, 1024], F32, 173568)
    tmpb1 = at("tmpb1", [128, 2048], BF16, 173568)
    tmpb2 = at("tmpb2", [128, 2048], BF16, 177664)
    tmp2 = at("tmp2", [128, 1024], F32, 177664)
    stag_f = at("stag_f", [128, 2048], F32, 181760)
    stag_i = at("stag_i", [128, 2048], I32, 189952)
    kvzl = at("kvzl", [128, 520], F32, 198144)
    kvz = at("kvz", [128, 520], F32, 200256)
    kvblk = at("kvblk", [128, 1024], BF16, 202368)
    zblk = at("zblk", [128, 128], BF16, 204416)
    selc = at("selc_s", [16, 1024], BF16, 204704)
    ident = at("ident_s", [128, 128], BF16, 206752)
    onesr = at("onesr_s", [1, 128], BF16, 207008)
    bias_s = at("bias_s", [1, 1024], BF16, 207264)
    stats = at("stats", [128, 176], F32, 209312)
    cols = at("cols", [128, 16], F32, 210016)
    C_SCOL, C_SMAX, C_SINV, C_QSEL, C_MQ, C_NEGMQ, C_NEG1, C_EPS, C_ONE, C_I1024, C_QT = range(11)
    assert 209952 + 64 <= 212863

    HSUM1, HSQ1, M1, RS1, SCR, SCRB, HSUM2, HSQ2, M2, R2, HSQ2B = 0, 16, 32, 48, 64, 80, 96, 112, 128, 144, 160

    es = contextlib.ExitStack()
    P0 = es.enter_context(nc.psum_tensor("P0", [128, 1024], F32))
    P1 = es.enter_context(nc.psum_tensor("P1", [128, 1024], F32))
    P2 = es.enter_context(nc.psum_tensor("P2", [128, 1024], F32))
    PK = es.enter_context(nc.psum_tensor("PK", [128, 640], F32))
    PS = PK

    SEMS = ("LC", "LCG", "LWG", "LX0", "LX1", "LW0", "LW1", "LCC", "GKV",
            "GX0", "GX1", "OUT0", "OUT1", "OUT2", "OUT3", "OUT4", "OUT5",
            "OUT6", "OUT7", "TAP", "TE", "VE", "AC", "GD", "CC")
    sem_hs = {s: es.enter_context(nc.semaphore(s)) for s in SEMS}

    r = _Rec()
    for s in SEMS:
        r.new_sem(s)

    def col(c):
        return cols[:, c:c + 1]

    def tap(name, src_tensor, wait):
        if name in dbg:
            r.emit("gpsimd", lambda e, s, o=dbg[name], i=src_tensor:
                   e.dma_start(out=o, in_=i[:, :]), waits=(wait,), inc=("TAP", 16),
                   dma=True)

    r.emit("vector", lambda e, s: e.memset(col(C_ONE), 1.0))
    r.emit("vector", lambda e, s: e.memset(col(C_NEG1), -1.0))
    r.emit("vector", lambda e, s: e.memset(col(C_EPS), EPS_LN))
    r.emit("vector", lambda e, s: e.memset(col(C_I1024), 1.0 / 1024.0))
    r.emit("vector", lambda e, s: e.tensor_scalar(
        out=col(C_QT), in0=col(C_ONE), scalar1=2.0 ** -30, scalar2=-1.0,
        op0=ALU.add, op1=ALU.add))
    r.emit("vector", lambda e, s: e.tensor_scalar(
        out=col(C_QSEL), in0=col(C_QT), scalar1=2.0 ** 30, scalar2=None,
        op0=ALU.mult))
    r.emit("vector", lambda e, s: e.tensor_scalar(
        out=col(C_MQ), in0=col(C_QSEL), scalar1=M64 - M32, scalar2=M32,
        op0=ALU.mult, op1=ALU.add))
    r.emit("vector", lambda e, s: e.tensor_scalar(
        out=col(C_NEGMQ), in0=col(C_MQ), scalar1=-1.0, scalar2=None,
        op0=ALU.mult), inc=("VE", 1))
    r.emit("vector", lambda e, s: e.memset(col(15), 0.0), inc=("VE", 1))
    ve_consts0 = r.cnt["VE"]
    r.emit("vector", lambda e, s: e.memset(PK[:, :], 0.0), inc=("VE", 1))
    ve_pk0 = r.cnt["VE"]
    r.emit("gpsimd", lambda e, s: e.memset(
        bass.AP(vsb, 64, [[16640, 128], [1040, 16], [65, 16]]), 1.0),
        inc=("GD", 1))
    gd_setup = r.cnt["GD"]

    def dma(eng, dst, src, sem="LC"):
        r.emit(eng, lambda e, s, d=dst, sr=src: e.dma_start(out=d, in_=sr),
               inc=(sem, 16), dma=True)

    ve_cast = {}
    te_tpose = {}
    ve_evacA = {}
    ve_vevac = {}
    ve_kf = {}
    ac_B = {}
    lxw = {}

    def emit_xload_A(t):
        xs_half = xstream[:, 1024 * (t % 2):1024 * (t % 2) + 1024]
        lx = ("LX0", "LX1")[t % 2]
        w = (ve_cast[t - 2],) if t - 2 in ve_cast else ()
        r.emit("sync", lambda e, s, d=xs_half, sr=x_h[128 * t:128 * (t + 1), :]:
               e.dma_start(out=d, in_=sr), waits=w, inc=(lx, 16))
        lxw[t] = (lx, r.cnt[lx])

    def emit_cast_A(t):
        xs_half = xstream[:, 1024 * (t % 2):1024 * (t % 2) + 1024]
        r.emit("scalar", lambda e, s, d=xbf[:, 1024 * t:1024 * (t + 1)], sr=xs_half:
               e.activation(out=d, in_=sr, func=AF.Identity, scale=col(C_ONE)),
               waits=(lxw[t],), inc=("AC", 1))
        ve_cast[t] = ("AC", r.cnt["AC"])


    emit_xload_A(0)
    emit_xload_A(1)
    dma("sync", ident[:, :], ident_h)
    dma("gpsimd", selc[:, :], selc_h, sem="LCG")
    dma("gpsimd", onesr[:, :], onesr_h, sem="LCG")
    dma("gpsimd", cols[:, C_SCOL:C_SCOL + 1], scol_h, sem="LCG")
    ld_consts = r.cnt["LC"]
    ld_constsg = r.cnt["LCG"]

    r.emit("vector", lambda e, s: e.tensor_scalar(
        out=col(C_SMAX), in0=col(C_SCOL), scalar1=1e-5, scalar2=None,
        op0=ALU.max), waits=(("LCG", ld_constsg),))
    r.emit("vector", lambda e, s: e.reciprocal(out=col(C_SINV), in_=col(C_SMAX)),
           inc=("VE", 1))

    dma("sync", stag_f[:, 0:1024], wqb_h[:, :], sem="LW0")
    ld_wq = r.cnt["LW0"]
    wqbf = at("wqbf", [128, 1024], BF16, 210080)
    r.emit("vector", lambda e, s: e.tensor_copy(out=wqbf[:, :], in_=stag_f[:, 0:1024]),
           waits=(("LW0", ld_wq),), inc=("VE", 1))
    ve_wq = r.cnt["VE"]
    r.emit("gpsimd", lambda e, s: e.dma_start(out=stag_f[:, :], in_=wkvb_h),
           waits=(("VE", ve_wq),), inc=("LWG", 16), dma=True)
    ld_wkv = r.cnt["LWG"]
    wkvbf = at("wkvbf", [128, 2048], BF16, off["stag_i"])
    r.emit("vector", lambda e, s: e.tensor_copy(out=wkvbf[:, :], in_=stag_f[:, :]),
           waits=(("LWG", ld_wkv),), inc=("VE", 1))
    ve_wkv = r.cnt["VE"]

    def emit_kv_mms(t):
        for p in range(NG):
            h0, h1 = 2 * p, 2 * p + 1
            l0 = kf[:, 1024 * t + 64 * h0:1024 * t + 64 * h0 + 64]
            l1 = kf[:, 1024 * t + 64 * h1:1024 * t + 64 * h1 + 64]
            r0 = vsb[:, 1040 * t + 65 * h0:1040 * t + 65 * h0 + 65]
            r1 = vsb[:, 1040 * t + 65 * h1:1040 * t + 65 * h1 + 65]
            sp0 = (t == NT - 1)
            pko = 65 * p if p < 7 else 512
            r.emit("tensor", lambda e, s, o=PK[0:64, pko:pko + 65], l=l0, rr=r0,
                   sp=sp0:
                   e.matmul(o, l, rr, start=False, stop=sp, tile_position=(0, 0),
                            skip_group_check=True),
                   waits=(("VE", ve_kf[t]),) if p == 0 else ())
            r.emit("tensor", lambda e, s, o=PK[64:128, pko:pko + 65], l=l1, rr=r1,
                   sp=sp0:
                   e.matmul(o, l, rr, start=False, stop=sp, tile_position=(0, 64),
                            skip_group_check=True),
                   inc=("TE", 1) if p == NG - 1 else None)

    def emit_tpose(t):
        slot = (P0, P2)[t % 2]
        waits = [ve_cast[t], ("LC", ld_consts), ("LCG", ld_constsg), ("VE", ve_pk0),
                 ("GD", gd_setup)]
        if t - 2 in ve_kf:
            waits.append(("VE", ve_kf[t - 2]))
        for g in range(NG):
            lhsT = xbf[:, 1024 * t + 128 * g:1024 * t + 128 * (g + 1)]
            r.emit("tensor", lambda e, s, o=slot[:, 128 * g:128 * (g + 1)], l=lhsT:
                   e.matmul(o, l, ident[:, :], start=True, stop=True),
                   waits=waits if g == 0 else (),
                   inc=("TE", 1) if g == NG - 1 else None)
        te_tpose[t] = r.cnt["TE"]

    def emit_evacA(t):
        slot = (P0, P2)[t % 2]
        r.emit("vector", lambda e, s, o=xfm[:, 1024 * t:1024 * (t + 1)], i=slot[:, :]:
               e.tensor_copy(out=o, in_=i), waits=(("TE", te_tpose[t]),),
               inc=("VE", 1))
        ve_evacA[t] = r.cnt["VE"]

    r.emit("scalar", None, waits=(("VE", ve_consts0),))

    emit_cast_A(0)
    emit_cast_A(1)
    emit_xload_A(2)
    emit_cast_A(2)
    emit_xload_A(3)
    emit_tpose(0)
    emit_evacA(0)
    for t in range(NT):
        slot = (P0, P2)[t % 2]
        waits = [("VE", max(ve_evacA[t], ve_wkv))]
        if t - 1 in ve_vevac:
            waits.append(("VE", ve_vevac[t - 1]))
        for g in range(NG):
            lhsT = xfm[:, 1024 * t + 128 * g:1024 * t + 128 * (g + 1)]
            rk = wkvbf[:, 256 * g:256 * g + 128]
            rv = wkvbf[:, 256 * g + 128:256 * g + 256]
            r.emit("tensor", lambda e, s, o=slot[:, 128 * g:128 * (g + 1)], l=lhsT, rr=rk:
                   e.matmul(o, l, rr, start=True, stop=True),
                   waits=waits if g == 0 else ())
            r.emit("tensor", lambda e, s, o=P1[:, 128 * g:128 * (g + 1)], l=lhsT, rr=rv:
                   e.matmul(o, l, rr, start=True, stop=True),
                   inc=("TE", 1) if g == NG - 1 else None)
        te_b = r.cnt["TE"]
        if t + 1 < NT:
            emit_tpose(t + 1)
        if t - 1 in ve_kf:
            emit_kv_mms(t - 1)
        tb = (tmpb1, tmpb2)[t % 2]
        r.emit("scalar", lambda e, s, P=slot, tb_=tb: e.activation(
            out=tb_[:, 0:1024], in_=P[:, :], func=AF.Exp, scale=col(C_ONE)),
            waits=(("TE", te_b),) + ((("VE", ve_kf[t - 2]),) if t - 2 in ve_kf else ()),
            inc=("AC", 1))
        ac_B[t] = r.cnt["AC"]
        if t + 4 < NT:
            emit_xload_A(t + 4)
        if t + 3 < NT:
            emit_cast_A(t + 3)
        if t + 1 < NT:
            emit_evacA(t + 1)
        v_out = bass.AP(vsb, 1040 * t, [[16640, 128], [65, 16], [1, 64]])
        v_in = bass.AP(P1, 0, [[1024, 128], [64, 16], [1, 64]])
        r.emit("vector", lambda e, s, o=v_out, i=v_in: e.tensor_copy(out=o, in_=i),
               waits=(("TE", te_b),), inc=("VE", 1))
        ve_vevac[t] = r.cnt["VE"]
        r.emit("vector", lambda e, s, tb_=tb: e.tensor_scalar(
            out=tb_[:, 1024:2048], in0=tb_[:, 0:1024], scalar1=1.0,
            scalar2=None, op0=ALU.min), waits=(("AC", ac_B[t]),))
        r.emit("vector", lambda e, s, P=slot, o=kf[:, 1024 * t:1024 * (t + 1)], tb_=tb:
               e.scalar_tensor_tensor(out=o, in0=P[:, :], scalar=0.0,
                                      in1=tb_[:, 1024:2048], op0=ALU.max, op1=ALU.add),
               inc=("VE", 1))
        ve_kf[t] = r.cnt["VE"]
    emit_kv_mms(NT - 1)
    ve_A_done = r.cnt["VE"]
    te_kv_done = r.cnt["TE"]
    tap("dbg_kf", kf, ("VE", ve_kf[NT - 1]))
    tap("dbg_v", vsb, ("VE", ve_vevac[NT - 1]))

    r.emit("vector", lambda e, s: e.tensor_copy(out=kvzl[:, 0:455], in_=PK[:, 0:455]),
           waits=(("TE", te_kv_done),))
    r.emit("vector", lambda e, s: e.tensor_copy(out=kvzl[:, 455:520], in_=PK[:, 512:577]),
           inc=("VE", 1))
    ve_kvzl = r.cnt["VE"]
    tap("dbg_kvzl", kvzl, ("VE", ve_kvzl))
    r.emit("gpsimd", lambda e, s: e.dma_start(out=cc_in.ap(), in_=kvzl[:, :]),
           waits=(("VE", ve_kvzl),), inc=("LCC", 16), dma=True)
    gd_ccin = r.cnt["LCC"]
    r.emit("gpsimd", lambda e, s: e.collective_compute(
        "AllReduce", ALU.add,
        replica_groups=[[0, 1, 2, 3], [4, 5, 6, 7]],
        ins=[cc_in.ap().opt()], outs=[cc_out.ap().opt()]),
        waits=(("LCC", gd_ccin),), inc=("CC", 1), dma=True)
    r.emit("gpsimd", lambda e, s: e.dma_start(out=kvz[:, :], in_=cc_out.ap()),
           waits=(("CC", r.cnt["CC"]),), inc=("GKV", 16), dma=True)
    gd_kvz = r.cnt["GKV"]
    tap("dbg_kvz", kvz, ("GKV", gd_kvz))

    ac_wo_cast = {}

    def emit_wo_chunk(g):
        wsrc = wo_h[128 * g:128 * (g + 1), :]
        half = stag_f[:, 1024 * (g % 2):1024 * (g % 2) + 1024]
        lw = ("LW0", "LW1")[g % 2]
        w = [("AC", ac_wo_cast[g - 2])] if g >= 2 else [("VE", ve_wkv)]
        r.emit("sync", lambda e, s, d=half, sr=wsrc: e.dma_start(out=d, in_=sr),
               waits=w, inc=(lw, 16))
        ldn = r.cnt[lw]
        r.emit("scalar", lambda e, s, o=wo_bf[:, 1024 * g:1024 * (g + 1)], i=half:
               e.activation(out=o, in_=i, func=AF.Identity, scale=col(C_ONE)),
               waits=((lw, ldn),), inc=("AC", 1))
        ac_wo_cast[g] = r.cnt["AC"]



    q_units = [(g, j2) for g in range(NG) for j2 in range(2)]
    ve_qf = {}
    ac_qe = {}
    te_q = {}

    def emit_qmm(u):
        g, j2 = q_units[u]
        P = (P0, P1, P2)[u % 3]
        waits = [("TE", te_kv_done), ("VE", ve_kf[NT - 1])]
        if u - 3 in ve_qf:
            waits.append(("VE", ve_qf[u - 3]))
        lhsT = wqbf[:, 128 * g:128 * (g + 1)]
        for jj in range(2):
            t0 = 8 * j2 + 4 * jj
            rhs = bass.AP(xfm, 1024 * t0 + 128 * g, [[16384, 128], [1024, 4], [1, 128]])
            r.emit("tensor", lambda e, s, P_=P, o0_=512 * jj, l=lhsT, rr=rhs:
                   e.matmul(P_[:, o0_:o0_ + 512], l, rr, start=True, stop=True),
                   waits=waits if jj == 0 else (),
                   inc=("TE", 1) if jj == 1 else None)
        te_q[u] = r.cnt["TE"]

    def emit_qexp(u):
        P = (P0, P1, P2)[u % 3]
        tb = (tmpb1, tmpb2)[u % 2]
        w = [("TE", te_q[u])]
        if u - 2 in ve_qf:
            w.append(("VE", ve_qf[u - 2]))
        r.emit("scalar", lambda e, s, P_=P, tb_=tb: e.activation(
            out=tb_[:, 0:1024], in_=P_[:, :], func=AF.Exp, scale=col(C_ONE)),
            waits=w, inc=("AC", 1))
        ac_qe[u] = r.cnt["AC"]

    def emit_qstt(u):
        g, j2 = q_units[u]
        P = (P0, P1, P2)[u % 3]
        tb = (tmpb1, tmpb2)[u % 2]
        qcols = qf[:, 2048 * g + 1024 * j2:2048 * g + 1024 * (j2 + 1)]
        r.emit("vector", lambda e, s, tb_=tb: e.tensor_scalar(
            out=tb_[:, 1024:2048], in0=tb_[:, 0:1024], scalar1=1.0,
            scalar2=None, op0=ALU.min), waits=(("AC", ac_qe[u]),))
        r.emit("vector", lambda e, s, P_=P, o=qcols, tb_=tb:
               e.scalar_tensor_tensor(out=o, in0=P_[:, :], scalar=0.0,
                                      in1=tb_[:, 1024:2048], op0=ALU.max, op1=ALU.add),
               inc=("VE", 1))
        ve_qf[u] = r.cnt["VE"]

    emit_qmm(0)
    emit_qexp(0)
    for u in range(16):
        if u + 1 < 16:
            emit_qmm(u + 1)
            emit_qexp(u + 1)
        emit_qstt(u)
        if u % 2 == 0:
            emit_wo_chunk(u // 2)
    ve_q_done = r.cnt["VE"]
    ve_wo_cast = ("AC", ac_wo_cast[NG - 1])

    ve_wt_cast = {}
    ve_bias_box = {}
    for g in range(NG):
        half = stag_i[:, 1024 * (g % 2):1024 * (g % 2) + 1024]
        lw = ("LW0", "LW1")[g % 2]
        if g < 2:
            w = [("VE", ve_kf[NT - 1]), ve_wo_cast]
        else:
            w = [("VE", ve_wt_cast[g - 2])]
        r.emit("sync", lambda e, s, d=half, sr=wt_h[128 * g:128 * (g + 1), :]:
               e.dma_start(out=d, in_=sr), waits=w, inc=(lw, 16))
        ldn = r.cnt[lw]
        r.emit("vector", lambda e, s, o=wt_bf[:, 1024 * g:1024 * (g + 1)], i=half:
               e.tensor_scalar(out=o, in0=i, scalar1=-1.0, scalar2=None,
                               op0=ALU.add), waits=((lw, ldn),), inc=("VE", 1))
        ve_wt_cast[g] = r.cnt["VE"]
    r.emit("sync", lambda e, s: e.dma_start(out=stag_f[0:1, 0:1024], in_=bias_h),
           waits=(("VE", ve_wt_cast[NG - 1]),), inc=("LW0", 16))
    ld_bias = r.cnt["LW0"]
    r.emit("vector", lambda e, s: e.tensor_scalar(
        out=bias_s[:, :], in0=stag_f[0:1, 0:1024],
        scalar1=cols[0:1, C_SINV:C_SINV + 1],
        scalar2=None, op0=ALU.mult), waits=(("LW0", ld_bias),), inc=("VE", 1))
    ve_bias_box[0] = r.cnt["VE"]

    r.emit("gpsimd", lambda e, s: e.memset(kvblk[:, :], 0.0), inc=("GD", 1))
    r.emit("gpsimd", lambda e, s: e.memset(zblk[:, :], 0.0), inc=("GD", 1))
    gd_blk = r.cnt["GD"]
    for p in range(NG):
        r.emit("vector", lambda e, s, p_=p: e.tensor_copy(
            out=kvblk[0:64, 128 * p_:128 * p_ + 64],
            in_=kvz[0:64, 65 * p_:65 * p_ + 64]),
            waits=(("GKV", gd_kvz), ("GD", gd_blk)) if p == 0 else ())
        r.emit("vector", lambda e, s, p_=p: e.tensor_copy(
            out=kvblk[64:128, 128 * p_ + 64:128 * p_ + 128],
            in_=kvz[64:128, 65 * p_:65 * p_ + 64]))
        r.emit("vector", lambda e, s, p_=p: e.tensor_copy(
            out=zblk[0:64, 16 * p_ + 2 * p_:16 * p_ + 2 * p_ + 1],
            in_=kvz[0:64, 65 * p_ + 64:65 * p_ + 65]))
        r.emit("vector", lambda e, s, p_=p: e.tensor_copy(
            out=zblk[64:128, 16 * p_ + 2 * p_ + 1:16 * p_ + 2 * p_ + 2],
            in_=kvz[64:128, 65 * p_ + 64:65 * p_ + 65]),
            inc=("VE", 1) if p == NG - 1 else None)
    ve_blk = r.cnt["VE"]
    tap("dbg_qf", qf, ("VE", ve_q_done))





    ve_denev = {}
    for j in range(4):
        PSd = (PS[0:16, 0:512], P2[0:16, 0:512])[j % 2]
        waits = [("VE", max(ve_blk, ve_q_done))]
        if j - 2 in ve_denev:
            waits.append(("VE", ve_denev[j - 2]))
        for p in range(NG):
            lhsT = zblk[:, 16 * p:16 * (p + 1)]
            rhs = qf[:, 2048 * p + 512 * j:2048 * p + 512 * (j + 1)]
            r.emit("tensor", lambda e, s, o=PSd, l=lhsT, rr=rhs,
                   st=(p == 0), sp=(p == NG - 1):
                   e.matmul(o, l, rr, start=st, stop=sp,
                            skip_group_check=True),
                   waits=waits if p == 0 else (),
                   inc=("TE", 1) if p == NG - 1 else None)
        ten = r.cnt["TE"]
        r.emit("vector", lambda e, s, o=den[:, 512 * j:512 * (j + 1)], i=PSd:
               e.tensor_scalar(out=o, in0=i, scalar1=EPS_DEN, scalar2=None,
                               op0=ALU.add), waits=(("TE", ten),), inc=("VE", 1))
        ve_denev[j] = r.cnt["VE"]
    ve_rdj = {}
    for j in range(4):
        r.emit("vector", lambda e, s, j_=j: e.reciprocal(
            out=rd[:, 512 * j_:512 * (j_ + 1)], in_=den[:, 512 * j_:512 * (j_ + 1)]))
        r.emit("vector", lambda e, s, j_=j: e.tensor_copy(
            out=rdb16[:, 512 * j_:512 * (j_ + 1)], in_=rd[:, 512 * j_:512 * (j_ + 1)]),
            inc=("VE", 1))
        ve_rdj[j] = r.cnt["VE"]
    ve_rd = r.cnt["VE"]
    tap("dbg_den", den, ("VE", ve_rd))

    ve_rdbe = {}
    ve_ctx = {}
    for u, (p, j) in enumerate([(p, j) for j in range(4) for p in range(NG)]):
        nhalf = P0[:, 512 * (u % 2):512 * (u % 2) + 512]
        rhalf = P1[:, 512 * (u % 2):512 * (u % 2) + 512]
        waits = [("VE", ve_rdj[j])]
        if u - 2 >= 0:
            waits.append(("VE", ve_ctx[u - 2]))
        qslice = qf[:, 2048 * p + 512 * j:2048 * p + 512 * (j + 1)]
        r.emit("tensor", lambda e, s, o=nhalf, l=kvblk[:, 128 * p:128 * (p + 1)], rr=qslice:
               e.matmul(o, l, rr, start=True, stop=True), waits=waits)
        r.emit("tensor", lambda e, s, o=rhalf, l=selc[:, 128 * p:128 * (p + 1)],
               rr=rdb16[:, 512 * j:512 * (j + 1)]:
               e.matmul(o, l, rr, start=True, stop=True), inc=("TE", 1))
        ten = r.cnt["TE"]
        rb = rdbe[:, 512 * (u % 2):512 * (u % 2) + 512]
        r.emit("scalar", lambda e, s, o=rb, i=rhalf: e.activation(
            out=o, in_=i, func=AF.Identity, scale=col(C_ONE)),
            waits=(("TE", ten),), inc=("AC", 1))
        ve_rdbe[u] = r.cnt["AC"]
        ccols = ctx[:, 2048 * p + 512 * j:2048 * p + 512 * (j + 1)]
        r.emit("vector", lambda e, s, o=ccols, a=nhalf, b=rb: e.tensor_mul(
            out=o, in0=a, in1=b), waits=(("AC", ve_rdbe[u]),), inc=("VE", 1))
        ve_ctx[u] = r.cnt["VE"]
    ve_ctx_done = r.cnt["VE"]
    tap("dbg_ctx", ctx, ("VE", ve_ctx_done))

    def t_tile(t):
        buf = t_lo if t < 8 else t_hi
        return buf[:, 1024 * (t % 8):1024 * (t % 8 + 1)]

    def ln_finalize(HSUM, HSQ, M, RS, rs_scale_col, lo=0, n=16):
        fam = lambda c: stats[:, c + lo:c + lo + n]
        r.emit("vector", lambda e, s: e.tensor_scalar(
            out=fam(SCR), in0=fam(HSQ), scalar1=1.0 / 1024.0, scalar2=None,
            op0=ALU.mult), waits=(("AC", r.cnt["AC"]),))
        r.emit("scalar", lambda e, s: e.activation(
            out=fam(M), in_=fam(HSUM), func=AF.Identity, scale=col(C_I1024)),
            inc=("AC", 1))
        acm = r.cnt["AC"]
        r.emit("vector", lambda e, s: e.tensor_mul(
            out=fam(SCRB), in0=fam(M), in1=fam(M)), waits=(("AC", acm),))
        r.emit("vector", lambda e, s: e.tensor_sub(
            out=fam(RS), in0=fam(SCR), in1=fam(SCRB)), inc=("VE", 1))
        vev = r.cnt["VE"]
        r.emit("scalar", lambda e, s: e.activation(
            out=fam(SCR), in_=fam(RS), func=AF.Sqrt, bias=col(C_EPS)),
            waits=(("VE", vev),), inc=("AC", 1))
        acs = r.cnt["AC"]
        r.emit("vector", lambda e, s: e.reciprocal(out=fam(SCRB), in_=fam(SCR)),
               waits=(("AC", acs),), inc=("VE", 1))
        vev = r.cnt["VE"]
        r.emit("scalar", lambda e, s, sc=rs_scale_col: e.activation(
            out=fam(RS), in_=fam(SCRB), func=AF.Identity, scale=sc),
            waits=(("VE", vev),))
        r.emit("scalar", lambda e, s: e.activation(
            out=fam(SCR), in_=fam(SCRB), func=AF.Identity, scale=col(C_ONE)),
            inc=("AC", 1))
        return r.cnt["AC"]

    gd_x = {}
    ve_resid = {}
    te_w = {}
    ac_sq1 = {}
    ac_rs1_half = {}

    def emit_xdma(t):
        gx = ("GX0", "GX1")[t % 2]
        w = (("VE", ve_resid[t - 2]),) if t - 2 in ve_resid else (("VE", ve_q_done),)
        r.emit("gpsimd", lambda e, s, d=xstream[:, 1024 * (t % 2):1024 * (t % 2) + 1024],
               sr=x_h[128 * t:128 * (t + 1), :]:
               e.dma_start(out=d, in_=sr), waits=w, inc=(gx, 16), dma=True)
        gd_x[t] = (gx, r.cnt[gx])

    emit_xdma(0)
    emit_xdma(1)
    for t in range(NT):
        PW = (P0, P1)[t % 2]
        waits = [("VE", ve_ctx_done), ve_wo_cast]
        if t - 2 in ve_resid:
            waits.append(("VE", ve_resid[t - 2]))
        for g in range(NG):
            lhsT = ctx[:, 2048 * g + 128 * t:2048 * g + 128 * (t + 1)]
            for jj in range(2):
                r.emit("tensor", lambda e, s, o=PW[:, 512 * jj:512 * (jj + 1)], l=lhsT,
                       rr=wo_bf[:, 1024 * g + 512 * jj:1024 * g + 512 * (jj + 1)],
                       st=(g == 0), sp=(g == NG - 1):
                       e.matmul(o, l, rr, start=st, stop=sp),
                       waits=waits if (g == 0 and jj == 0) else (),
                       inc=("TE", 1) if (g == NG - 1 and jj == 1) else None)
        te_w[t] = r.cnt["TE"]
        r.emit("vector", lambda e, s, P_=PW, xt=xstream[:, 1024 * (t % 2):1024 * (t % 2) + 1024],
               o=t_tile(t), hc=stats[:, HSUM1 + t:HSUM1 + t + 1]:
               e.scalar_tensor_tensor(out=o, in0=P_[:, :], scalar=0.0, in1=xt,
                                      op0=ALU.add, op1=ALU.add, accum_out=hc),
               waits=(("TE", te_w[t]), gd_x[t]), inc=("VE", 1))
        ve_resid[t] = r.cnt["VE"]
        if "dbg_h" in dbg:
            r.emit("gpsimd", lambda e, s, o=dbg["dbg_h"][:, 1024 * t:1024 * (t + 1)],
                   i=t_tile(t): e.dma_start(out=o, in_=i[:, :]),
                   waits=(("VE", ve_resid[t]),), inc=("OUT", 16), dma=True)
        if t + 2 < NT:
            emit_xdma(t + 2)
        r.emit("scalar", lambda e, s, i=t_tile(t), hc=stats[:, HSQ1 + t:HSQ1 + t + 1]:
               e.activation(out=tmp2[:, :], in_=i, func=AF.Square,
                            accum_out=hc), waits=(("VE", ve_resid[t]),),
               inc=("AC", 1))
        ac_sq1[t] = r.cnt["AC"]
        if t == 7 or t == 15:
            r.emit("vector", None, waits=(("AC", ac_sq1[t]),))
            ac_rs1_half[t // 8] = ln_finalize(HSUM1, HSQ1, M1, RS1, col(C_SINV),
                                              lo=8 * (t // 8), n=8)

    ve_wt_done = ve_wt_cast[NG - 1]
    ve_bias = ve_bias_box[0]



    ac_rs1 = ac_rs1_half[1]

    ve_quant = {}
    te_xT = {}
    ac_xint_ev = {}
    ve_h2 = {}
    ac_sq2 = {}
    ve_x2 = {}
    te_ffn = {}

    def emit_quant(t):
        qw = ()
        if t == 0:
            qw = (("AC", ac_rs1_half[0]),)
        elif t == 8:
            qw = (("AC", ac_rs1_half[1]),)
        r.emit("vector", lambda e, s, c=t_tile(t), m=stats[:, M1 + t:M1 + t + 1],
               rs=stats[:, RS1 + t:RS1 + t + 1]:
               e.tensor_scalar(out=tmp2[:, :], in0=c, scalar1=m, scalar2=rs,
                               op0=ALU.subtract, op1=ALU.mult),
               waits=qw)
        r.emit("vector", lambda e, s:
               e.tensor_scalar(out=tmp1[:, :], in0=tmp2[:, :], scalar1=127.0,
                               scalar2=-127.0, op0=ALU.min, op1=ALU.max))
        r.emit("vector", lambda e, s, o=xint_tm[:, 1024 * t:1024 * (t + 1)]:
               e.tensor_scalar(out=o, in0=tmp1[:, :], scalar1=col(C_MQ),
                               scalar2=col(C_NEGMQ), op0=ALU.add, op1=ALU.add),
               inc=("VE", 1))
        ve_quant[t] = r.cnt["VE"]

    def emit_x2(t):
        xb = xstream[:, 1024 * (t % 2):1024 * (t % 2) + 1024]
        r.emit("vector", lambda e, s, i=t_tile(t), o=xb,
               m=stats[:, M1 + t:M1 + t + 1], ri=stats[:, SCR + t:SCR + t + 1]:
               e.tensor_scalar(out=o, in0=i, scalar1=m, scalar2=ri,
                               op0=ALU.subtract, op1=ALU.mult),
               inc=("VE", 1))
        ve_x2[t] = r.cnt["VE"]

    def emit_ffn(t):
        PF = (P0, P1)[t % 2]
        wv = max(ve_wt_done, ve_bias)
        waits = [("VE", wv), ("AC", ac_xint_ev[t]), ("LC", ld_consts)]
        if t - 2 in ve_h2:
            waits.append(("VE", ve_h2[t - 2]))
        for g in range(NG):
            lhsT = xint_fm[:, 2048 * g + 128 * t:2048 * g + 128 * (t + 1)]
            for jj in range(2):
                r.emit("tensor", lambda e, s, o=PF[:, 512 * jj:512 * (jj + 1)], l=lhsT,
                       rr=wt_bf[:, 1024 * g + 512 * jj:1024 * g + 512 * (jj + 1)],
                       st=(g == 0):
                       e.matmul(o, l, rr, start=st, stop=False,
                                skip_group_check=True),
                       waits=waits if (g == 0 and jj == 0) else ())
        for jj in range(2):
            r.emit("tensor", lambda e, s, o=PF[:, 512 * jj:512 * (jj + 1)],
                   rr=bias_s[0:1, 512 * jj:512 * (jj + 1)]:
                   e.matmul(o, onesr[:, :], rr, start=False, stop=True,
                            skip_group_check=True),
                   inc=("TE", 1) if jj == 1 else None)
        te_ffn[t] = r.cnt["TE"]

    def emit_h2(t):
        PF = (P0, P1)[t % 2]
        xb = xstream[:, 1024 * (t % 2):1024 * (t % 2) + 1024]
        r.emit("vector", lambda e, s, P_=PF, o=t_tile(t), xb_=xb,
               hc=stats[:, HSUM2 + t:HSUM2 + t + 1]:
               e.scalar_tensor_tensor(out=o, in0=P_[:, :], scalar=col(C_SMAX),
                                      in1=xb_, op0=ALU.mult, op1=ALU.add,
                                      accum_out=hc),
               waits=(("TE", te_ffn[t]), ("VE", ve_x2[t])), inc=("VE", 1))
        ve_h2[t] = r.cnt["VE"]
        r.emit("scalar", lambda e, s, i=t_tile(t)[:, 0:512],
               hc=stats[:, HSQ2 + t:HSQ2 + t + 1]:
               e.activation(out=PK[0:128, 0:512], in_=i, func=AF.Square,
                            accum_out=hc),
               waits=(("VE", ve_h2[t]),))
        r.emit("scalar", lambda e, s, i=t_tile(t)[:, 512:1024],
               hc=stats[:, HSQ2B + t:HSQ2B + t + 1]:
               e.activation(out=PK[0:128, 0:512], in_=i, func=AF.Square,
                            accum_out=hc),
               inc=("AC", 1))
        ac_sq2[t] = r.cnt["AC"]

    ve_out = {}
    gd_out = {}
    ac_r2h = {}

    def emit_ln2_half(h):
        lo = 8 * h
        r.emit("vector", lambda e, s, lo_=lo: e.tensor_add(
            out=stats[:, RS1 + lo_:RS1 + lo_ + 8],
            in0=stats[:, HSQ2B + lo_:HSQ2B + lo_ + 8],
            in1=stats[:, HSQ2 + lo_:HSQ2 + lo_ + 8]),
            waits=(("AC", ac_sq2[lo + 7]),), inc=("VE", 1))
        ac_r2h[h] = ln_finalize(HSUM2, RS1, M2, R2, col(C_ONE), lo=lo, n=8)

    def emit_out(t):
        ob = outstage[:, 1024 * (t % 8):1024 * (t % 8) + 1024]
        w2 = [("AC", ac_r2h[t // 8])]
        if t < 8:
            w2.append(("TE", te_xT[min(2 * t + 1, NT - 1)]))
        if t - 8 in gd_out:
            w2.append(gd_out[t - 8])
        r.emit("vector", lambda e, s, i=t_tile(t),
               m=stats[:, M2 + t:M2 + t + 1], r2=stats[:, R2 + t:R2 + t + 1], o=ob:
               e.tensor_scalar(out=o, in0=i, scalar1=m, scalar2=r2,
                               op0=ALU.subtract, op1=ALU.mult),
               waits=w2, inc=("VE", 1))
        ve_out[t] = r.cnt["VE"]
        qeng = ("gpsimd", "sync")[t % 2]
        ou = f"OUT{t % 8}"
        r.emit(qeng, lambda e, s, o=out_h[128 * t:128 * (t + 1), :], i=ob:
               e.dma_start(out=o, in_=i), waits=(("VE", ve_out[t]),),
               inc=(ou, 16), dma=True)
        gd_out[t] = (ou, r.cnt[ou])


    ln2_hooks = {}
    emit_quant(0)
    emit_x2(0)
    for t in range(NT):
        waits = [("VE", ve_quant[t])]
        if t - 1 in ac_xint_ev:
            waits.append(("AC", ac_xint_ev[t - 1]))
        if t == 0:
            waits.append(("AC", r.cnt["AC"]))
        for g in range(NG):
            lhsT = xint_tm[:, 1024 * t + 128 * g:1024 * t + 128 * (g + 1)]
            r.emit("tensor", lambda e, s, o=P2[:, 128 * g:128 * (g + 1)], l=lhsT:
                   e.matmul(o, l, ident[:, :], start=True, stop=True),
                   waits=waits if g == 0 else (),
                   inc=("TE", 1) if g == NG - 1 else None)
        te_xT[t] = r.cnt["TE"]
        if t - 1 >= 0:
            emit_ffn(t - 1)
        if t + 1 < NT:
            emit_quant(t + 1)
        out_ap = bass.AP(xint_fm, 128 * t, [[16384, 128], [2048, 8], [1, 128]])
        in_ap = bass.AP(P2, 0, [[1024, 128], [128, 8], [1, 128]])
        r.emit("scalar", lambda e, s, o=out_ap, i=in_ap: e.activation(
            out=o, in_=i, func=AF.Identity, scale=col(C_ONE)),
            waits=(("TE", te_xT[t]),), inc=("AC", 1))
        ac_xint_ev[t] = r.cnt["AC"]
        if t - 1 >= 0:
            emit_x2(t)
            emit_h2(t - 1)
        if t == 9:
            emit_ln2_half(0)
        if 10 <= t <= 15:
            emit_out(t - 10)
    emit_ffn(NT - 1)
    emit_h2(NT - 1)
    for tt in (6, 7):
        emit_out(tt)
    emit_ln2_half(1)
    for tt in range(8, NT):
        emit_out(tt)

    tap("dbg_xint", xint_tm, ("VE", ve_quant[NT - 1]))
    tap("dbg_xintfm", xint_fm, ("AC", ac_xint_ev[NT - 1]))
    tap("dbg_wtbf", wt_bf, ("VE", ve_wt_done))
    tap("dbg_stats", stats, ("VE", ve_out[NT - 1]))

    if "dbg_t" in debug_taps:
        for tt in range(NT):
            r.emit("gpsimd", lambda e, s, o=dbg["dbg_t"], tt_=tt:
                   e.dma_start(out=o[:, 1024 * tt_:1024 * (tt_ + 1)], in_=t_tile(tt_)[:, :]),
                   waits=(("VE", ve_quant[tt]),), inc=("OUT", 16), dma=True)

    outsems = [f"OUT{i}" for i in range(8)] + ["TAP"]
    r.wait_all("gpsimd", *outsems)
    r.wait_all("sync", *outsems)

    with nc.Block() as block:
        @block.sync
        def _(e):
            r.replay("sync", e, sem_hs)

        @block.gpsimd
        def _(e):
            r.replay("gpsimd", e, sem_hs)

        @block.vector
        def _(e):
            r.replay("vector", e, sem_hs)

        @block.scalar
        def _(e):
            r.replay("scalar", e, sem_hs)

        @block.tensor
        def _(e):
            r.replay("tensor", e, sem_hs)

    es.close()
    return nc


_cached = {}


def _get_nc():
    key = tuple(_DEBUG_TAPS)
    if key not in _cached:
        _cached[key] = build_nc(key)
    return _cached[key]


def kernel(L0_attn_query, L0_attn_Wq, L0_attn_Wk, L0_attn_Wv, L0_attn_Wo,
           L0_ffn_ternary_weight, L0_ffn_bias, L0_ffn_act_scale):
    x = np.ascontiguousarray(np.asarray(L0_attn_query, dtype=np.float32))
    Wq = np.asarray(L0_attn_Wq, dtype=np.float32)
    Wk = np.asarray(L0_attn_Wk, dtype=np.float32)
    Wv = np.asarray(L0_attn_Wv, dtype=np.float32)
    Wo = np.ascontiguousarray(np.asarray(L0_attn_Wo, dtype=np.float32))
    wt = np.ascontiguousarray(np.asarray(L0_ffn_ternary_weight, dtype=np.int32).T)
    bias = np.ascontiguousarray(np.asarray(L0_ffn_bias, dtype=np.float32).reshape(1, E))
    scal = np.asarray(L0_ffn_act_scale, dtype=np.float32).reshape(1, 1)

    wqb = np.zeros((128, 1024), np.float32)
    wkvb = np.zeros((128, 2048), np.float32)
    for g in range(NG):
        wqb[0:64, 128 * g:128 * g + 64] = Wq[2 * g]
        wqb[64:128, 128 * g + 64:128 * g + 128] = Wq[2 * g + 1]
        wkvb[0:64, 256 * g:256 * g + 64] = Wk[2 * g]
        wkvb[64:128, 256 * g + 64:256 * g + 128] = Wk[2 * g + 1]
        wkvb[0:64, 256 * g + 128:256 * g + 192] = Wv[2 * g]
        wkvb[64:128, 256 * g + 192:256 * g + 256] = Wv[2 * g + 1]
    scol = np.broadcast_to(scal, (128, 1)).copy()
    ident = np.eye(128, dtype=ml_dtypes.bfloat16)
    selc = np.zeros((16, 1024), ml_dtypes.bfloat16)
    for p in range(NG):
        selc[2 * p, 128 * p:128 * p + 64] = 1
        selc[2 * p + 1, 128 * p + 64:128 * p + 128] = 1
    onesr = np.ones((1, 128), ml_dtypes.bfloat16)

    base = {"wqb": wqb, "wkvb": wkvb, "wo": Wo, "wt": wt, "bias": bias,
            "scol": scol, "ident": ident, "selc": selc, "onesr": onesr}
    in_maps = []
    for c in range(NCORE):
        b, q = divmod(c, 4)
        m = dict(base)
        m["x"] = np.ascontiguousarray(x[b, T * q:T * (q + 1), :])
        in_maps.append(m)

    nc = _get_nc()
    res = run_bass_kernel_spmd(nc, in_maps, core_ids=list(range(NCORE)))

    out = np.empty((B, S, E), np.float32)
    for c in range(NCORE):
        b, q = divmod(c, 4)
        out[b, T * q:T * (q + 1), :] = res.results[c]["out"]
    if _DEBUG_TAPS:
        kernel.last_results = res.results
    return out


# version 30
# speedup vs baseline: 1.0599x; 1.0599x over previous
import contextlib
import numpy as np
import ml_dtypes

import concourse.bass as bass
import concourse.mybir as mybir
from concourse.bass_utils import run_bass_kernel_spmd

F32 = mybir.dt.float32
BF16 = mybir.dt.bfloat16
I32 = mybir.dt.int32
ALU = mybir.AluOpType
AF = mybir.ActivationFunctionType

NCORE = 8
B, S, E, H, Dh = 2, 8192, 1024, 16, 64
T = 2048
NT = 16
NG = 8
M32 = float(np.float32(1.5 * 2 ** 23))
M64 = float(1.5 * 2 ** 52)
EPS_DEN = 1e-6
EPS_LN = 1e-5

_DEBUG_TAPS = ()


_CHAINED_ENGINES = {"gpsimd", "vector", "scalar"}


class _Rec:
    ENGINES = ("sync", "gpsimd", "vector", "scalar", "tensor")

    def __init__(self):
        self.ops = {e: [] for e in self.ENGINES}
        self.cnt = {}

    def new_sem(self, name):
        self.cnt[name] = 0

    CHAIN = {"vector": "VE", "scalar": "AC", "gpsimd": "GD"}

    def emit(self, eng, fn, waits=(), inc=None, dma=False):
        waits = list(waits)
        ch = self.CHAIN.get(eng)
        if ch is not None and not dma and fn is not None:
            if eng in _CHAINED_ENGINES and self.cnt[ch] > 0:
                waits.append((ch, self.cnt[ch]))
            if inc is None:
                inc = (ch, 1)
            elif inc[0] != ch:
                raise ValueError(f"op on {eng} must inc {ch}, got {inc}")
        if inc is not None:
            self.cnt[inc[0]] += inc[1]
        self.ops[eng].append((fn, tuple(waits), inc))

    def wait_all(self, eng, *sems):
        w = tuple((s, self.cnt[s]) for s in sems if self.cnt[s] > 0)
        if w:
            self.emit(eng, None, waits=w)

    def replay(self, eng_name, e, sems):
        for fn, waits, inc in self.ops[eng_name]:
            for (s, v) in waits:
                e.wait_ge(sems[s], v)
            if fn is None:
                continue
            ins = fn(e, sems)
            if inc is not None:
                ins.then_inc(sems[inc[0]], inc[1])


def build_nc(debug_taps=(), detect_races=False):
    nc = bass.Bass("TRN2", target_bir_lowering=False,
                   detect_race_conditions=detect_races)

    x_h = nc.dram_tensor("x", [T, E], F32, kind="ExternalInput").ap()
    wqb_h = nc.dram_tensor("wqb", [128, 1024], F32, kind="ExternalInput").ap()
    wkvb_h = nc.dram_tensor("wkvb", [128, 2048], F32, kind="ExternalInput").ap()
    wo_h = nc.dram_tensor("wo", [E, E], F32, kind="ExternalInput").ap()
    wt_h = nc.dram_tensor("wt", [E, E], I32, kind="ExternalInput").ap()
    bias_h = nc.dram_tensor("bias", [1, E], F32, kind="ExternalInput").ap()
    scol_h = nc.dram_tensor("scol", [128, 1], F32, kind="ExternalInput").ap()
    ident_h = nc.dram_tensor("ident", [128, 128], BF16, kind="ExternalInput").ap()
    selc_h = nc.dram_tensor("selc", [16, 1024], BF16, kind="ExternalInput").ap()
    onesr_h = nc.dram_tensor("onesr", [1, 128], BF16, kind="ExternalInput").ap()
    out_h = nc.dram_tensor("out", [T, E], F32, kind="ExternalOutput").ap()
    cc_in = nc.dram_tensor("cc_in", [128, 520], F32)
    cc_out = nc.dram_tensor("cc_out", [128, 520], F32)

    dbg = {}
    for name, shape, dt in (
        ("dbg_xfm", [128, 16384], BF16), ("dbg_kf", [128, 16384], BF16),
        ("dbg_v", [128, 16640], BF16), ("dbg_kvz", [128, 520], F32),
        ("dbg_qf", [128, 16384], BF16), ("dbg_ctx", [128, 16384], BF16),
        ("dbg_den", [16, 2048], F32), ("dbg_t", [128, 16384], F32),
        ("dbg_xint", [128, 16384], BF16), ("dbg_h2", [128, 2048], F32),
        ("dbg_kvzl", [128, 520], F32), ("dbg_stats", [128, 160], F32),
        ("dbg_h", [128, 16384], F32),
        ("dbg_xintfm", [128, 16384], BF16), ("dbg_wtbf", [128, 8192], BF16),
    ):
        if name in debug_taps:
            dbg[name] = nc.dram_tensor(name, shape, dt, kind="ExternalOutput").ap()

    A = nc.alloc_sbuf_tensor_at
    off = {}
    SB0 = 16512

    def at(name, shape, dt, o):
        off[name] = o
        return A(name, shape, dt, offset=SB0 + o)

    S1 = 0
    S2 = 33280
    S3 = 66560
    S4 = 99840

    xbf = at("xbf", [128, 16384], BF16, S1)
    qf = at("qf", [128, 16384], BF16, S1)
    xint_tm = at("xint_tm", [128, 16384], BF16, S1)
    outstage = at("outstage", [128, 8192], F32, S1)
    kf = at("kf", [128, 16384], BF16, S2)
    ctx = at("ctx", [128, 16384], BF16, S2)
    xint_fm = at("xint_fm", [128, 16384], BF16, S2)
    vsb = at("vsb", [128, 16640], BF16, S3)
    t_lo = at("t_lo", [128, 8192], F32, S3)
    xfm = at("xfm", [128, 16384], BF16, S4)
    den = at("den", [16, 2048], F32, S4)
    rd = at("rd", [16, 2048], F32, S4 + 8192)
    rdb16 = at("rdb16", [16, 2048], BF16, S4 + 16384)
    numbf = at("numbf", [128, 1024], BF16, S4 + 20480)
    rdbe = at("rdbe", [128, 1024], BF16, S4 + 22528)
    t_hi = at("t_hi", [128, 8192], F32, S4)

    wo_bf = at("wo_bf", [128, 8192], BF16, 132608)
    wt_bf = at("wt_bf", [128, 8192], BF16, 148992)
    xstream = at("xstream", [128, 2048], F32, 165376)
    tmp1 = at("tmp1", [128, 1024], F32, 173568)
    tmpb1 = at("tmpb1", [128, 2048], BF16, 173568)
    tmpb2 = at("tmpb2", [128, 2048], BF16, 177664)
    tmp2 = at("tmp2", [128, 1024], F32, 177664)
    stag_f = at("stag_f", [128, 2048], F32, 181760)
    stag_i = at("stag_i", [128, 2048], I32, 189952)
    kvzl = at("kvzl", [128, 520], F32, 198144)
    kvz = at("kvz", [128, 520], F32, 200256)
    kvblk = at("kvblk", [128, 1024], BF16, 202368)
    zblk = at("zblk", [128, 128], BF16, 204416)
    selc = at("selc_s", [16, 1024], BF16, 204704)
    ident = at("ident_s", [128, 128], BF16, 206752)
    onesr = at("onesr_s", [1, 128], BF16, 207008)
    bias_s = at("bias_s", [1, 1024], BF16, 207264)
    stats = at("stats", [128, 176], F32, 209312)
    cols = at("cols", [128, 16], F32, 210016)
    C_SCOL, C_SMAX, C_SINV, C_QSEL, C_MQ, C_NEGMQ, C_NEG1, C_EPS, C_ONE, C_I1024, C_QT = range(11)
    assert 209952 + 64 <= 212863

    HSUM1, HSQ1, M1, RS1, SCR, SCRB, HSUM2, HSQ2, M2, R2, HSQ2B = 0, 16, 32, 48, 64, 80, 96, 112, 128, 144, 160

    es = contextlib.ExitStack()
    P0 = es.enter_context(nc.psum_tensor("P0", [128, 1024], F32))
    P1 = es.enter_context(nc.psum_tensor("P1", [128, 1024], F32))
    P2 = es.enter_context(nc.psum_tensor("P2", [128, 1024], F32))
    PK = es.enter_context(nc.psum_tensor("PK", [128, 640], F32))
    PS = PK

    SEMS = ("LC", "LCG", "LWG", "LX0", "LX1", "LW0", "LW1", "LCC", "GKV",
            "GX0", "GX1", "OUT0", "OUT1", "OUT2", "OUT3", "OUT4", "OUT5",
            "OUT6", "OUT7", "TAP", "TE", "VE", "AC", "GD", "CC")
    sem_hs = {s: es.enter_context(nc.semaphore(s)) for s in SEMS}

    r = _Rec()
    for s in SEMS:
        r.new_sem(s)

    def col(c):
        return cols[:, c:c + 1]

    def tap(name, src_tensor, wait):
        if name in dbg:
            r.emit("gpsimd", lambda e, s, o=dbg[name], i=src_tensor:
                   e.dma_start(out=o, in_=i[:, :]), waits=(wait,), inc=("TAP", 16),
                   dma=True)

    r.emit("vector", lambda e, s: e.memset(col(C_ONE), 1.0))
    r.emit("vector", lambda e, s: e.memset(col(C_NEG1), -1.0))
    r.emit("vector", lambda e, s: e.memset(col(C_EPS), EPS_LN))
    r.emit("vector", lambda e, s: e.memset(col(C_I1024), 1.0 / 1024.0))
    r.emit("vector", lambda e, s: e.tensor_scalar(
        out=col(C_QT), in0=col(C_ONE), scalar1=2.0 ** -30, scalar2=-1.0,
        op0=ALU.add, op1=ALU.add))
    r.emit("vector", lambda e, s: e.tensor_scalar(
        out=col(C_QSEL), in0=col(C_QT), scalar1=2.0 ** 30, scalar2=None,
        op0=ALU.mult))
    r.emit("vector", lambda e, s: e.tensor_scalar(
        out=col(C_MQ), in0=col(C_QSEL), scalar1=M64 - M32, scalar2=M32,
        op0=ALU.mult, op1=ALU.add))
    r.emit("vector", lambda e, s: e.tensor_scalar(
        out=col(C_NEGMQ), in0=col(C_MQ), scalar1=-1.0, scalar2=None,
        op0=ALU.mult), inc=("VE", 1))
    r.emit("vector", lambda e, s: e.memset(col(15), 0.0), inc=("VE", 1))
    ve_consts0 = r.cnt["VE"]
    r.emit("vector", lambda e, s: e.memset(PK[:, :], 0.0), inc=("VE", 1))
    ve_pk0 = r.cnt["VE"]
    r.emit("gpsimd", lambda e, s: e.memset(
        bass.AP(vsb, 64, [[16640, 128], [1040, 16], [65, 16]]), 1.0),
        inc=("GD", 1))
    gd_setup = r.cnt["GD"]

    def dma(eng, dst, src, sem="LC"):
        r.emit(eng, lambda e, s, d=dst, sr=src: e.dma_start(out=d, in_=sr),
               inc=(sem, 16), dma=True)

    ve_cast = {}
    te_tpose = {}
    ve_evacA = {}
    ve_vevac = {}
    ve_kf = {}
    ac_B = {}
    lxw = {}

    def emit_xload_A(t):
        xs_half = xstream[:, 1024 * (t % 2):1024 * (t % 2) + 1024]
        lx = ("LX0", "LX1")[t % 2]
        w = (ve_cast[t - 2],) if t - 2 in ve_cast else ()
        r.emit("sync", lambda e, s, d=xs_half, sr=x_h[128 * t:128 * (t + 1), :]:
               e.dma_start(out=d, in_=sr), waits=w, inc=(lx, 16))
        lxw[t] = (lx, r.cnt[lx])

    def emit_cast_A(t):
        xs_half = xstream[:, 1024 * (t % 2):1024 * (t % 2) + 1024]
        r.emit("scalar", lambda e, s, d=xbf[:, 1024 * t:1024 * (t + 1)], sr=xs_half:
               e.activation(out=d, in_=sr, func=AF.Identity, scale=col(C_ONE)),
               waits=(lxw[t],), inc=("AC", 1))
        ve_cast[t] = ("AC", r.cnt["AC"])


    emit_xload_A(0)
    emit_xload_A(1)
    dma("sync", ident[:, :], ident_h)
    dma("gpsimd", selc[:, :], selc_h, sem="LCG")
    dma("gpsimd", onesr[:, :], onesr_h, sem="LCG")
    dma("gpsimd", cols[:, C_SCOL:C_SCOL + 1], scol_h, sem="LCG")
    ld_consts = r.cnt["LC"]
    ld_constsg = r.cnt["LCG"]

    r.emit("vector", lambda e, s: e.tensor_scalar(
        out=col(C_SMAX), in0=col(C_SCOL), scalar1=1e-5, scalar2=None,
        op0=ALU.max), waits=(("LCG", ld_constsg),))
    r.emit("vector", lambda e, s: e.reciprocal(out=col(C_SINV), in_=col(C_SMAX)),
           inc=("VE", 1))

    dma("sync", stag_f[:, 0:1024], wqb_h[:, :], sem="LW0")
    ld_wq = r.cnt["LW0"]
    wqbf = at("wqbf", [128, 1024], BF16, 210080)
    r.emit("vector", lambda e, s: e.tensor_copy(out=wqbf[:, :], in_=stag_f[:, 0:1024]),
           waits=(("LW0", ld_wq),), inc=("VE", 1))
    ve_wq = r.cnt["VE"]
    r.emit("gpsimd", lambda e, s: e.dma_start(out=stag_f[:, :], in_=wkvb_h),
           waits=(("VE", ve_wq),), inc=("LWG", 16), dma=True)
    ld_wkv = r.cnt["LWG"]
    wkvbf = at("wkvbf", [128, 2048], BF16, off["stag_i"])
    r.emit("vector", lambda e, s: e.tensor_copy(out=wkvbf[:, :], in_=stag_f[:, :]),
           waits=(("LWG", ld_wkv),), inc=("VE", 1))
    ve_wkv = r.cnt["VE"]

    def emit_kv_mms(t):
        for p in range(NG):
            h0, h1 = 2 * p, 2 * p + 1
            l0 = kf[:, 1024 * t + 64 * h0:1024 * t + 64 * h0 + 64]
            l1 = kf[:, 1024 * t + 64 * h1:1024 * t + 64 * h1 + 64]
            r0 = vsb[:, 1040 * t + 65 * h0:1040 * t + 65 * h0 + 65]
            r1 = vsb[:, 1040 * t + 65 * h1:1040 * t + 65 * h1 + 65]
            sp0 = (t == NT - 1)
            pko = 65 * p if p < 7 else 512
            r.emit("tensor", lambda e, s, o=PK[0:64, pko:pko + 65], l=l0, rr=r0,
                   sp=sp0:
                   e.matmul(o, l, rr, start=False, stop=sp, tile_position=(0, 0),
                            skip_group_check=True),
                   waits=(("VE", ve_kf[t]),) if p == 0 else ())
            r.emit("tensor", lambda e, s, o=PK[64:128, pko:pko + 65], l=l1, rr=r1,
                   sp=sp0:
                   e.matmul(o, l, rr, start=False, stop=sp, tile_position=(0, 64),
                            skip_group_check=True),
                   inc=("TE", 1) if p == NG - 1 else None)

    def emit_tpose(t):
        slot = (P0, P2)[t % 2]
        waits = [ve_cast[t], ("LC", ld_consts), ("LCG", ld_constsg), ("VE", ve_pk0),
                 ("GD", gd_setup)]
        if t - 2 in ve_kf:
            waits.append(("VE", ve_kf[t - 2]))
        for g in range(NG):
            lhsT = xbf[:, 1024 * t + 128 * g:1024 * t + 128 * (g + 1)]
            r.emit("tensor", lambda e, s, o=slot[:, 128 * g:128 * (g + 1)], l=lhsT:
                   e.matmul(o, l, ident[:, :], start=True, stop=True),
                   waits=waits if g == 0 else (),
                   inc=("TE", 1) if g == NG - 1 else None)
        te_tpose[t] = r.cnt["TE"]

    def emit_evacA(t):
        slot = (P0, P2)[t % 2]
        r.emit("vector", lambda e, s, o=xfm[:, 1024 * t:1024 * (t + 1)], i=slot[:, :]:
               e.tensor_copy(out=o, in_=i), waits=(("TE", te_tpose[t]),),
               inc=("VE", 1))
        ve_evacA[t] = r.cnt["VE"]

    r.emit("scalar", None, waits=(("VE", ve_consts0),))

    emit_cast_A(0)
    emit_cast_A(1)
    emit_xload_A(2)
    emit_cast_A(2)
    emit_xload_A(3)
    emit_tpose(0)
    emit_evacA(0)
    for t in range(NT):
        slot = (P0, P2)[t % 2]
        waits = [("VE", max(ve_evacA[t], ve_wkv))]
        if t - 1 in ve_vevac:
            waits.append(("VE", ve_vevac[t - 1]))
        for g in range(NG):
            lhsT = xfm[:, 1024 * t + 128 * g:1024 * t + 128 * (g + 1)]
            rk = wkvbf[:, 256 * g:256 * g + 128]
            rv = wkvbf[:, 256 * g + 128:256 * g + 256]
            r.emit("tensor", lambda e, s, o=slot[:, 128 * g:128 * (g + 1)], l=lhsT, rr=rk:
                   e.matmul(o, l, rr, start=True, stop=True),
                   waits=waits if g == 0 else ())
            r.emit("tensor", lambda e, s, o=P1[:, 128 * g:128 * (g + 1)], l=lhsT, rr=rv:
                   e.matmul(o, l, rr, start=True, stop=True),
                   inc=("TE", 1) if g == NG - 1 else None)
        te_b = r.cnt["TE"]
        if t + 1 < NT:
            emit_tpose(t + 1)
        if t - 1 in ve_kf:
            emit_kv_mms(t - 1)
        tb = (tmpb1, tmpb2)[t % 2]
        r.emit("scalar", lambda e, s, P=slot, tb_=tb: e.activation(
            out=tb_[:, 0:1024], in_=P[:, :], func=AF.Exp, scale=col(C_ONE)),
            waits=(("TE", te_b),) + ((("VE", ve_kf[t - 2]),) if t - 2 in ve_kf else ()),
            inc=("AC", 1))
        ac_B[t] = r.cnt["AC"]
        if t + 4 < NT:
            emit_xload_A(t + 4)
        if t + 3 < NT:
            emit_cast_A(t + 3)
        if t + 1 < NT:
            emit_evacA(t + 1)
        v_out = bass.AP(vsb, 1040 * t, [[16640, 128], [65, 16], [1, 64]])
        v_in = bass.AP(P1, 0, [[1024, 128], [64, 16], [1, 64]])
        r.emit("vector", lambda e, s, o=v_out, i=v_in: e.tensor_copy(out=o, in_=i),
               waits=(("TE", te_b),), inc=("VE", 1))
        ve_vevac[t] = r.cnt["VE"]
        r.emit("vector", lambda e, s, tb_=tb: e.tensor_scalar(
            out=tb_[:, 1024:2048], in0=tb_[:, 0:1024], scalar1=1.0,
            scalar2=None, op0=ALU.min), waits=(("AC", ac_B[t]),))
        r.emit("vector", lambda e, s, P=slot, o=kf[:, 1024 * t:1024 * (t + 1)], tb_=tb:
               e.scalar_tensor_tensor(out=o, in0=P[:, :], scalar=0.0,
                                      in1=tb_[:, 1024:2048], op0=ALU.max, op1=ALU.add),
               inc=("VE", 1))
        ve_kf[t] = r.cnt["VE"]
    emit_kv_mms(NT - 1)
    ve_A_done = r.cnt["VE"]
    te_kv_done = r.cnt["TE"]
    tap("dbg_kf", kf, ("VE", ve_kf[NT - 1]))
    tap("dbg_v", vsb, ("VE", ve_vevac[NT - 1]))

    r.emit("vector", lambda e, s: e.tensor_copy(out=kvzl[:, 0:455], in_=PK[:, 0:455]),
           waits=(("TE", te_kv_done),))
    r.emit("vector", lambda e, s: e.tensor_copy(out=kvzl[:, 455:520], in_=PK[:, 512:577]),
           inc=("VE", 1))
    ve_kvzl = r.cnt["VE"]
    tap("dbg_kvzl", kvzl, ("VE", ve_kvzl))
    r.emit("gpsimd", lambda e, s: e.dma_start(out=cc_in.ap(), in_=kvzl[:, :]),
           waits=(("VE", ve_kvzl),), inc=("LCC", 16), dma=True)
    gd_ccin = r.cnt["LCC"]
    r.emit("gpsimd", lambda e, s: e.collective_compute(
        "AllReduce", ALU.add,
        replica_groups=[[0, 1, 2, 3], [4, 5, 6, 7]],
        ins=[cc_in.ap().opt()], outs=[cc_out.ap().opt()]),
        waits=(("LCC", gd_ccin),), inc=("CC", 1), dma=True)
    r.emit("gpsimd", lambda e, s: e.dma_start(out=kvz[:, :], in_=cc_out.ap()),
           waits=(("CC", r.cnt["CC"]),), inc=("GKV", 16), dma=True)
    gd_kvz = r.cnt["GKV"]
    tap("dbg_kvz", kvz, ("GKV", gd_kvz))

    ac_wo_cast = {}

    def emit_wo_chunk(g):
        wsrc = wo_h[128 * g:128 * (g + 1), :]
        half = stag_f[:, 1024 * (g % 2):1024 * (g % 2) + 1024]
        lw = ("LW0", "LW1")[g % 2]
        w = [("AC", ac_wo_cast[g - 2])] if g >= 2 else [("VE", ve_wkv)]
        r.emit("sync", lambda e, s, d=half, sr=wsrc: e.dma_start(out=d, in_=sr),
               waits=w, inc=(lw, 16))
        ldn = r.cnt[lw]
        r.emit("scalar", lambda e, s, o=wo_bf[:, 1024 * g:1024 * (g + 1)], i=half:
               e.activation(out=o, in_=i, func=AF.Identity, scale=col(C_ONE)),
               waits=((lw, ldn),), inc=("AC", 1))
        ac_wo_cast[g] = r.cnt["AC"]



    q_units = [(g, j2) for g in range(NG) for j2 in range(2)]
    ve_qf = {}
    ac_qe = {}
    te_q = {}

    def emit_qmm(u):
        g, j2 = q_units[u]
        P = (P0, P1, P2)[u % 3]
        waits = [("TE", te_kv_done), ("VE", ve_kf[NT - 1])]
        if u - 3 in ve_qf:
            waits.append(("VE", ve_qf[u - 3]))
        lhsT = wqbf[:, 128 * g:128 * (g + 1)]
        for jj in range(2):
            t0 = 8 * j2 + 4 * jj
            rhs = bass.AP(xfm, 1024 * t0 + 128 * g, [[16384, 128], [1024, 4], [1, 128]])
            r.emit("tensor", lambda e, s, P_=P, o0_=512 * jj, l=lhsT, rr=rhs:
                   e.matmul(P_[:, o0_:o0_ + 512], l, rr, start=True, stop=True),
                   waits=waits if jj == 0 else (),
                   inc=("TE", 1) if jj == 1 else None)
        te_q[u] = r.cnt["TE"]

    def emit_qexp(u):
        P = (P0, P1, P2)[u % 3]
        tb = (tmpb1, tmpb2)[u % 2]
        w = [("TE", te_q[u])]
        if u - 2 in ve_qf:
            w.append(("VE", ve_qf[u - 2]))
        r.emit("scalar", lambda e, s, P_=P, tb_=tb: e.activation(
            out=tb_[:, 0:1024], in_=P_[:, :], func=AF.Exp, scale=col(C_ONE)),
            waits=w, inc=("AC", 1))
        ac_qe[u] = r.cnt["AC"]

    def emit_qstt(u):
        g, j2 = q_units[u]
        P = (P0, P1, P2)[u % 3]
        tb = (tmpb1, tmpb2)[u % 2]
        qcols = qf[:, 2048 * g + 1024 * j2:2048 * g + 1024 * (j2 + 1)]
        r.emit("vector", lambda e, s, tb_=tb: e.tensor_scalar(
            out=tb_[:, 1024:2048], in0=tb_[:, 0:1024], scalar1=1.0,
            scalar2=None, op0=ALU.min), waits=(("AC", ac_qe[u]),))
        r.emit("vector", lambda e, s, P_=P, o=qcols, tb_=tb:
               e.scalar_tensor_tensor(out=o, in0=P_[:, :], scalar=0.0,
                                      in1=tb_[:, 1024:2048], op0=ALU.max, op1=ALU.add),
               inc=("VE", 1))
        ve_qf[u] = r.cnt["VE"]

    emit_qmm(0)
    emit_qexp(0)
    for u in range(16):
        if u + 1 < 16:
            emit_qmm(u + 1)
            emit_qexp(u + 1)
        emit_qstt(u)
        if u % 2 == 0:
            emit_wo_chunk(u // 2)
    ve_q_done = r.cnt["VE"]
    ve_wo_cast = ("AC", ac_wo_cast[NG - 1])

    r.emit("gpsimd", lambda e, s: e.memset(kvblk[:, :], 0.0), inc=("GD", 1))
    r.emit("gpsimd", lambda e, s: e.memset(zblk[:, :], 0.0), inc=("GD", 1))
    gd_blk = r.cnt["GD"]
    for p in range(NG):
        r.emit("vector", lambda e, s, p_=p: e.tensor_copy(
            out=zblk[0:64, 16 * p_ + 2 * p_:16 * p_ + 2 * p_ + 1],
            in_=kvz[0:64, 65 * p_ + 64:65 * p_ + 65]),
            waits=(("GKV", gd_kvz), ("GD", gd_blk)) if p == 0 else ())
        r.emit("vector", lambda e, s, p_=p: e.tensor_copy(
            out=zblk[64:128, 16 * p_ + 2 * p_ + 1:16 * p_ + 2 * p_ + 2],
            in_=kvz[64:128, 65 * p_ + 64:65 * p_ + 65]),
            inc=("VE", 1) if p == NG - 1 else None)
    ve_zblk = r.cnt["VE"]
    for p in range(NG):
        r.emit("vector", lambda e, s, p_=p: e.tensor_copy(
            out=kvblk[0:64, 128 * p_:128 * p_ + 64],
            in_=kvz[0:64, 65 * p_:65 * p_ + 64]))
        r.emit("vector", lambda e, s, p_=p: e.tensor_copy(
            out=kvblk[64:128, 128 * p_ + 64:128 * p_ + 128],
            in_=kvz[64:128, 65 * p_:65 * p_ + 64]),
            inc=("VE", 1) if p == NG - 1 else None)
    ve_blk = r.cnt["VE"]
    tap("dbg_qf", qf, ("VE", ve_q_done))





    ve_denev = {}
    for j in range(4):
        PSd = (PS[0:16, 0:512], P2[0:16, 0:512])[j % 2]
        waits = [("VE", max(ve_zblk, ve_q_done))]
        if j - 2 in ve_denev:
            waits.append(("VE", ve_denev[j - 2]))
        for p in range(NG):
            lhsT = zblk[:, 16 * p:16 * (p + 1)]
            rhs = qf[:, 2048 * p + 512 * j:2048 * p + 512 * (j + 1)]
            r.emit("tensor", lambda e, s, o=PSd, l=lhsT, rr=rhs,
                   st=(p == 0), sp=(p == NG - 1):
                   e.matmul(o, l, rr, start=st, stop=sp,
                            skip_group_check=True),
                   waits=waits if p == 0 else (),
                   inc=("TE", 1) if p == NG - 1 else None)
        ten = r.cnt["TE"]
        r.emit("vector", lambda e, s, o=den[:, 512 * j:512 * (j + 1)], i=PSd:
               e.tensor_scalar(out=o, in0=i, scalar1=EPS_DEN, scalar2=None,
                               op0=ALU.add), waits=(("TE", ten),), inc=("VE", 1))
        ve_denev[j] = r.cnt["VE"]
    ve_rdj = {}
    for j in range(4):
        r.emit("vector", lambda e, s, j_=j: e.reciprocal(
            out=rd[:, 512 * j_:512 * (j_ + 1)], in_=den[:, 512 * j_:512 * (j_ + 1)]))
        r.emit("vector", lambda e, s, j_=j: e.tensor_copy(
            out=rdb16[:, 512 * j_:512 * (j_ + 1)], in_=rd[:, 512 * j_:512 * (j_ + 1)]),
            inc=("VE", 1))
        ve_rdj[j] = r.cnt["VE"]
    ve_rd = r.cnt["VE"]
    tap("dbg_den", den, ("VE", ve_rd))

    ve_rdbe = {}
    ve_ctx = {}
    for u, (p, j) in enumerate([(p, j) for j in range(4) for p in range(NG)]):
        nhalf = P0[:, 512 * (u % 2):512 * (u % 2) + 512]
        rhalf = P1[:, 512 * (u % 2):512 * (u % 2) + 512]
        waits = [("VE", ve_rdj[j])]
        if u - 2 >= 0:
            waits.append(("VE", ve_ctx[u - 2]))
        qslice = qf[:, 2048 * p + 512 * j:2048 * p + 512 * (j + 1)]
        r.emit("tensor", lambda e, s, o=nhalf, l=kvblk[:, 128 * p:128 * (p + 1)], rr=qslice:
               e.matmul(o, l, rr, start=True, stop=True), waits=waits)
        r.emit("tensor", lambda e, s, o=rhalf, l=selc[:, 128 * p:128 * (p + 1)],
               rr=rdb16[:, 512 * j:512 * (j + 1)]:
               e.matmul(o, l, rr, start=True, stop=True), inc=("TE", 1))
        ten = r.cnt["TE"]
        rb = rdbe[:, 512 * (u % 2):512 * (u % 2) + 512]
        r.emit("scalar", lambda e, s, o=rb, i=rhalf: e.activation(
            out=o, in_=i, func=AF.Identity, scale=col(C_ONE)),
            waits=(("TE", ten),), inc=("AC", 1))
        ve_rdbe[u] = r.cnt["AC"]
        ccols = ctx[:, 2048 * p + 512 * j:2048 * p + 512 * (j + 1)]
        r.emit("vector", lambda e, s, o=ccols, a=nhalf, b=rb: e.tensor_mul(
            out=o, in0=a, in1=b), waits=(("AC", ve_rdbe[u]),), inc=("VE", 1))
        ve_ctx[u] = r.cnt["VE"]
    ve_ctx_done = r.cnt["VE"]
    tap("dbg_ctx", ctx, ("VE", ve_ctx_done))

    def t_tile(t):
        buf = t_lo if t < 8 else t_hi
        return buf[:, 1024 * (t % 8):1024 * (t % 8 + 1)]

    def ln_finalize(HSUM, HSQ, M, RS, rs_scale_col, lo=0, n=16):
        fam = lambda c: stats[:, c + lo:c + lo + n]
        r.emit("vector", lambda e, s: e.tensor_scalar(
            out=fam(SCR), in0=fam(HSQ), scalar1=1.0 / 1024.0, scalar2=None,
            op0=ALU.mult), waits=(("AC", r.cnt["AC"]),))
        r.emit("scalar", lambda e, s: e.activation(
            out=fam(M), in_=fam(HSUM), func=AF.Identity, scale=col(C_I1024)),
            inc=("AC", 1))
        acm = r.cnt["AC"]
        r.emit("vector", lambda e, s: e.tensor_mul(
            out=fam(SCRB), in0=fam(M), in1=fam(M)), waits=(("AC", acm),))
        r.emit("vector", lambda e, s: e.tensor_sub(
            out=fam(RS), in0=fam(SCR), in1=fam(SCRB)), inc=("VE", 1))
        vev = r.cnt["VE"]
        r.emit("scalar", lambda e, s: e.activation(
            out=fam(SCR), in_=fam(RS), func=AF.Sqrt, bias=col(C_EPS)),
            waits=(("VE", vev),), inc=("AC", 1))
        acs = r.cnt["AC"]
        r.emit("vector", lambda e, s: e.reciprocal(out=fam(SCRB), in_=fam(SCR)),
               waits=(("AC", acs),), inc=("VE", 1))
        vev = r.cnt["VE"]
        r.emit("scalar", lambda e, s, sc=rs_scale_col: e.activation(
            out=fam(RS), in_=fam(SCRB), func=AF.Identity, scale=sc),
            waits=(("VE", vev),))
        r.emit("scalar", lambda e, s: e.activation(
            out=fam(SCR), in_=fam(SCRB), func=AF.Identity, scale=col(C_ONE)),
            inc=("AC", 1))
        return r.cnt["AC"]

    gd_x = {}
    ve_resid = {}
    te_w = {}
    ac_sq1 = {}
    ve_wt_cast = {}
    ve_bias_box = {}
    ac_rs1_half = {}

    def emit_xdma(t):
        gx = ("GX0", "GX1")[t % 2]
        w = (("VE", ve_resid[t - 2]),) if t - 2 in ve_resid else (("VE", ve_q_done),)
        r.emit("gpsimd", lambda e, s, d=xstream[:, 1024 * (t % 2):1024 * (t % 2) + 1024],
               sr=x_h[128 * t:128 * (t + 1), :]:
               e.dma_start(out=d, in_=sr), waits=w, inc=(gx, 16), dma=True)
        gd_x[t] = (gx, r.cnt[gx])

    emit_xdma(0)
    emit_xdma(1)
    for t in range(NT):
        PW = (P0, P1)[t % 2]
        waits = [("VE", ve_ctx_done), ve_wo_cast]
        if t - 2 in ve_resid:
            waits.append(("VE", ve_resid[t - 2]))
        for g in range(NG):
            lhsT = ctx[:, 2048 * g + 128 * t:2048 * g + 128 * (t + 1)]
            for jj in range(2):
                r.emit("tensor", lambda e, s, o=PW[:, 512 * jj:512 * (jj + 1)], l=lhsT,
                       rr=wo_bf[:, 1024 * g + 512 * jj:1024 * g + 512 * (jj + 1)],
                       st=(g == 0), sp=(g == NG - 1):
                       e.matmul(o, l, rr, start=st, stop=sp),
                       waits=waits if (g == 0 and jj == 0) else (),
                       inc=("TE", 1) if (g == NG - 1 and jj == 1) else None)
        te_w[t] = r.cnt["TE"]
        r.emit("vector", lambda e, s, P_=PW, xt=xstream[:, 1024 * (t % 2):1024 * (t % 2) + 1024],
               o=t_tile(t), hc=stats[:, HSUM1 + t:HSUM1 + t + 1]:
               e.scalar_tensor_tensor(out=o, in0=P_[:, :], scalar=0.0, in1=xt,
                                      op0=ALU.add, op1=ALU.add, accum_out=hc),
               waits=(("TE", te_w[t]), gd_x[t]), inc=("VE", 1))
        ve_resid[t] = r.cnt["VE"]
        if "dbg_h" in dbg:
            r.emit("gpsimd", lambda e, s, o=dbg["dbg_h"][:, 1024 * t:1024 * (t + 1)],
                   i=t_tile(t): e.dma_start(out=o, in_=i[:, :]),
                   waits=(("VE", ve_resid[t]),), inc=("OUT", 16), dma=True)
        if t + 2 < NT:
            emit_xdma(t + 2)
        r.emit("scalar", lambda e, s, i=t_tile(t), hc=stats[:, HSQ1 + t:HSQ1 + t + 1]:
               e.activation(out=tmp2[:, :], in_=i, func=AF.Square,
                            accum_out=hc), waits=(("VE", ve_resid[t]),),
               inc=("AC", 1))
        ac_sq1[t] = r.cnt["AC"]
        if t == 7 or t == 15:
            r.emit("vector", None, waits=(("AC", ac_sq1[t]),))
            ac_rs1_half[t // 8] = ln_finalize(HSUM1, HSQ1, M1, RS1, col(C_SINV),
                                              lo=8 * (t // 8), n=8)
        if t < NG:
            g = t
            half = stag_i[:, 1024 * (g % 2):1024 * (g % 2) + 1024]
            lw = ("LW0", "LW1")[g % 2]
            if g < 2:
                w = [("VE", ve_kf[NT - 1]), ve_wo_cast]
            else:
                w = [("VE", ve_wt_cast[g - 2])]
            r.emit("sync", lambda e, s, d=half, sr=wt_h[128 * g:128 * (g + 1), :]:
                   e.dma_start(out=d, in_=sr), waits=w, inc=(lw, 16))
            ldn = r.cnt[lw]
            r.emit("vector", lambda e, s, o=wt_bf[:, 1024 * g:1024 * (g + 1)], i=half:
                   e.tensor_scalar(out=o, in0=i, scalar1=-1.0, scalar2=None,
                                   op0=ALU.add), waits=((lw, ldn),), inc=("VE", 1))
            ve_wt_cast[g] = r.cnt["VE"]
        elif t == NG:
            r.emit("sync", lambda e, s: e.dma_start(out=stag_f[0:1, 0:1024], in_=bias_h),
                   waits=(("VE", ve_wt_cast[NG - 1]),), inc=("LW0", 16))
            ld_bias = r.cnt["LW0"]
            r.emit("vector", lambda e, s: e.tensor_scalar(
                out=bias_s[:, :], in0=stag_f[0:1, 0:1024],
                scalar1=cols[0:1, C_SINV:C_SINV + 1],
                scalar2=None, op0=ALU.mult), waits=(("LW0", ld_bias),), inc=("VE", 1))
            ve_bias_box[0] = r.cnt["VE"]

    ve_wt_done = ve_wt_cast[NG - 1]
    ve_bias = ve_bias_box[0]



    ac_rs1 = ac_rs1_half[1]

    ve_quant = {}
    te_xT = {}
    ac_xint_ev = {}
    ve_h2 = {}
    ac_sq2 = {}
    ve_x2 = {}
    te_ffn = {}

    def emit_quant(t):
        qw = ()
        if t == 0:
            qw = (("AC", ac_rs1_half[0]),)
        elif t == 8:
            qw = (("AC", ac_rs1_half[1]),)
        r.emit("vector", lambda e, s, c=t_tile(t), m=stats[:, M1 + t:M1 + t + 1],
               rs=stats[:, RS1 + t:RS1 + t + 1]:
               e.tensor_scalar(out=tmp2[:, :], in0=c, scalar1=m, scalar2=rs,
                               op0=ALU.subtract, op1=ALU.mult),
               waits=qw)
        r.emit("vector", lambda e, s:
               e.tensor_scalar(out=tmp1[:, :], in0=tmp2[:, :], scalar1=127.0,
                               scalar2=-127.0, op0=ALU.min, op1=ALU.max))
        r.emit("vector", lambda e, s, o=xint_tm[:, 1024 * t:1024 * (t + 1)]:
               e.tensor_scalar(out=o, in0=tmp1[:, :], scalar1=col(C_MQ),
                               scalar2=col(C_NEGMQ), op0=ALU.add, op1=ALU.add),
               inc=("VE", 1))
        ve_quant[t] = r.cnt["VE"]

    def emit_x2(t):
        xb = xstream[:, 1024 * (t % 2):1024 * (t % 2) + 1024]
        r.emit("vector", lambda e, s, i=t_tile(t), o=xb,
               m=stats[:, M1 + t:M1 + t + 1], ri=stats[:, SCR + t:SCR + t + 1]:
               e.tensor_scalar(out=o, in0=i, scalar1=m, scalar2=ri,
                               op0=ALU.subtract, op1=ALU.mult),
               inc=("VE", 1))
        ve_x2[t] = r.cnt["VE"]

    def emit_ffn(t):
        PF = (P0, P1)[t % 2]
        wv = max(ve_wt_done, ve_bias)
        waits = [("VE", wv), ("AC", ac_xint_ev[t]), ("LC", ld_consts)]
        if t - 2 in ve_h2:
            waits.append(("VE", ve_h2[t - 2]))
        for g in range(NG):
            lhsT = xint_fm[:, 2048 * g + 128 * t:2048 * g + 128 * (t + 1)]
            for jj in range(2):
                r.emit("tensor", lambda e, s, o=PF[:, 512 * jj:512 * (jj + 1)], l=lhsT,
                       rr=wt_bf[:, 1024 * g + 512 * jj:1024 * g + 512 * (jj + 1)],
                       st=(g == 0):
                       e.matmul(o, l, rr, start=st, stop=False,
                                skip_group_check=True),
                       waits=waits if (g == 0 and jj == 0) else ())
        for jj in range(2):
            r.emit("tensor", lambda e, s, o=PF[:, 512 * jj:512 * (jj + 1)],
                   rr=bias_s[0:1, 512 * jj:512 * (jj + 1)]:
                   e.matmul(o, onesr[:, :], rr, start=False, stop=True,
                            skip_group_check=True),
                   inc=("TE", 1) if jj == 1 else None)
        te_ffn[t] = r.cnt["TE"]

    def emit_h2(t):
        PF = (P0, P1)[t % 2]
        xb = xstream[:, 1024 * (t % 2):1024 * (t % 2) + 1024]
        r.emit("vector", lambda e, s, P_=PF, o=t_tile(t), xb_=xb,
               hc=stats[:, HSUM2 + t:HSUM2 + t + 1]:
               e.scalar_tensor_tensor(out=o, in0=P_[:, :], scalar=col(C_SMAX),
                                      in1=xb_, op0=ALU.mult, op1=ALU.add,
                                      accum_out=hc),
               waits=(("TE", te_ffn[t]), ("VE", ve_x2[t])), inc=("VE", 1))
        ve_h2[t] = r.cnt["VE"]
        r.emit("scalar", lambda e, s, i=t_tile(t)[:, 0:512],
               hc=stats[:, HSQ2 + t:HSQ2 + t + 1]:
               e.activation(out=PK[0:128, 0:512], in_=i, func=AF.Square,
                            accum_out=hc),
               waits=(("VE", ve_h2[t]),))
        r.emit("scalar", lambda e, s, i=t_tile(t)[:, 512:1024],
               hc=stats[:, HSQ2B + t:HSQ2B + t + 1]:
               e.activation(out=PK[0:128, 0:512], in_=i, func=AF.Square,
                            accum_out=hc),
               inc=("AC", 1))
        ac_sq2[t] = r.cnt["AC"]

    ve_out = {}
    gd_out = {}
    ac_r2h = {}

    def emit_ln2_half(h):
        lo = 8 * h
        r.emit("vector", lambda e, s, lo_=lo: e.tensor_add(
            out=stats[:, RS1 + lo_:RS1 + lo_ + 8],
            in0=stats[:, HSQ2B + lo_:HSQ2B + lo_ + 8],
            in1=stats[:, HSQ2 + lo_:HSQ2 + lo_ + 8]),
            waits=(("AC", ac_sq2[lo + 7]),), inc=("VE", 1))
        ac_r2h[h] = ln_finalize(HSUM2, RS1, M2, R2, col(C_ONE), lo=lo, n=8)

    def emit_out(t):
        ob = outstage[:, 1024 * (t % 8):1024 * (t % 8) + 1024]
        w2 = [("AC", ac_r2h[t // 8])]
        if t < 8:
            w2.append(("TE", te_xT[min(2 * t + 1, NT - 1)]))
        if t - 8 in gd_out:
            w2.append(gd_out[t - 8])
        r.emit("vector", lambda e, s, i=t_tile(t),
               m=stats[:, M2 + t:M2 + t + 1], r2=stats[:, R2 + t:R2 + t + 1], o=ob:
               e.tensor_scalar(out=o, in0=i, scalar1=m, scalar2=r2,
                               op0=ALU.subtract, op1=ALU.mult),
               waits=w2, inc=("VE", 1))
        ve_out[t] = r.cnt["VE"]
        qeng = ("gpsimd", "sync")[t % 2]
        ou = f"OUT{t % 8}"
        r.emit(qeng, lambda e, s, o=out_h[128 * t:128 * (t + 1), :], i=ob:
               e.dma_start(out=o, in_=i), waits=(("VE", ve_out[t]),),
               inc=(ou, 16), dma=True)
        gd_out[t] = (ou, r.cnt[ou])


    ln2_hooks = {}
    emit_quant(0)
    emit_x2(0)
    for t in range(NT):
        waits = [("VE", ve_quant[t])]
        if t - 1 in ac_xint_ev:
            waits.append(("AC", ac_xint_ev[t - 1]))
        if t == 0:
            waits.append(("AC", r.cnt["AC"]))
        for g in range(NG):
            lhsT = xint_tm[:, 1024 * t + 128 * g:1024 * t + 128 * (g + 1)]
            r.emit("tensor", lambda e, s, o=P2[:, 128 * g:128 * (g + 1)], l=lhsT:
                   e.matmul(o, l, ident[:, :], start=True, stop=True),
                   waits=waits if g == 0 else (),
                   inc=("TE", 1) if g == NG - 1 else None)
        te_xT[t] = r.cnt["TE"]
        if t - 1 >= 0:
            emit_ffn(t - 1)
        if t + 1 < NT:
            emit_quant(t + 1)
        out_ap = bass.AP(xint_fm, 128 * t, [[16384, 128], [2048, 8], [1, 128]])
        in_ap = bass.AP(P2, 0, [[1024, 128], [128, 8], [1, 128]])
        r.emit("scalar", lambda e, s, o=out_ap, i=in_ap: e.activation(
            out=o, in_=i, func=AF.Identity, scale=col(C_ONE)),
            waits=(("TE", te_xT[t]),), inc=("AC", 1))
        ac_xint_ev[t] = r.cnt["AC"]
        if t - 1 >= 0:
            emit_x2(t)
            emit_h2(t - 1)
        if t == 9:
            emit_ln2_half(0)
        if 10 <= t <= 15:
            emit_out(t - 10)
    emit_ffn(NT - 1)
    emit_h2(NT - 1)
    for tt in (6, 7):
        emit_out(tt)
    emit_ln2_half(1)
    for tt in range(8, NT):
        emit_out(tt)

    tap("dbg_xint", xint_tm, ("VE", ve_quant[NT - 1]))
    tap("dbg_xintfm", xint_fm, ("AC", ac_xint_ev[NT - 1]))
    tap("dbg_wtbf", wt_bf, ("VE", ve_wt_done))
    tap("dbg_stats", stats, ("VE", ve_out[NT - 1]))

    if "dbg_t" in debug_taps:
        for tt in range(NT):
            r.emit("gpsimd", lambda e, s, o=dbg["dbg_t"], tt_=tt:
                   e.dma_start(out=o[:, 1024 * tt_:1024 * (tt_ + 1)], in_=t_tile(tt_)[:, :]),
                   waits=(("VE", ve_quant[tt]),), inc=("OUT", 16), dma=True)

    outsems = [f"OUT{i}" for i in range(8)] + ["TAP"]
    r.wait_all("gpsimd", *outsems)
    r.wait_all("sync", *outsems)

    with nc.Block() as block:
        @block.sync
        def _(e):
            r.replay("sync", e, sem_hs)

        @block.gpsimd
        def _(e):
            r.replay("gpsimd", e, sem_hs)

        @block.vector
        def _(e):
            r.replay("vector", e, sem_hs)

        @block.scalar
        def _(e):
            r.replay("scalar", e, sem_hs)

        @block.tensor
        def _(e):
            r.replay("tensor", e, sem_hs)

    es.close()
    return nc


_cached = {}


def _get_nc():
    key = tuple(_DEBUG_TAPS)
    if key not in _cached:
        _cached[key] = build_nc(key)
    return _cached[key]


def kernel(L0_attn_query, L0_attn_Wq, L0_attn_Wk, L0_attn_Wv, L0_attn_Wo,
           L0_ffn_ternary_weight, L0_ffn_bias, L0_ffn_act_scale):
    x = np.ascontiguousarray(np.asarray(L0_attn_query, dtype=np.float32))
    Wq = np.asarray(L0_attn_Wq, dtype=np.float32)
    Wk = np.asarray(L0_attn_Wk, dtype=np.float32)
    Wv = np.asarray(L0_attn_Wv, dtype=np.float32)
    Wo = np.ascontiguousarray(np.asarray(L0_attn_Wo, dtype=np.float32))
    wt = np.ascontiguousarray(np.asarray(L0_ffn_ternary_weight, dtype=np.int32).T)
    bias = np.ascontiguousarray(np.asarray(L0_ffn_bias, dtype=np.float32).reshape(1, E))
    scal = np.asarray(L0_ffn_act_scale, dtype=np.float32).reshape(1, 1)

    wqb = np.zeros((128, 1024), np.float32)
    wkvb = np.zeros((128, 2048), np.float32)
    for g in range(NG):
        wqb[0:64, 128 * g:128 * g + 64] = Wq[2 * g]
        wqb[64:128, 128 * g + 64:128 * g + 128] = Wq[2 * g + 1]
        wkvb[0:64, 256 * g:256 * g + 64] = Wk[2 * g]
        wkvb[64:128, 256 * g + 64:256 * g + 128] = Wk[2 * g + 1]
        wkvb[0:64, 256 * g + 128:256 * g + 192] = Wv[2 * g]
        wkvb[64:128, 256 * g + 192:256 * g + 256] = Wv[2 * g + 1]
    scol = np.broadcast_to(scal, (128, 1)).copy()
    ident = np.eye(128, dtype=ml_dtypes.bfloat16)
    selc = np.zeros((16, 1024), ml_dtypes.bfloat16)
    for p in range(NG):
        selc[2 * p, 128 * p:128 * p + 64] = 1
        selc[2 * p + 1, 128 * p + 64:128 * p + 128] = 1
    onesr = np.ones((1, 128), ml_dtypes.bfloat16)

    base = {"wqb": wqb, "wkvb": wkvb, "wo": Wo, "wt": wt, "bias": bias,
            "scol": scol, "ident": ident, "selc": selc, "onesr": onesr}
    in_maps = []
    for c in range(NCORE):
        b, q = divmod(c, 4)
        m = dict(base)
        m["x"] = np.ascontiguousarray(x[b, T * q:T * (q + 1), :])
        in_maps.append(m)

    nc = _get_nc()
    res = run_bass_kernel_spmd(nc, in_maps, core_ids=list(range(NCORE)))

    out = np.empty((B, S, E), np.float32)
    for c in range(NCORE):
        b, q = divmod(c, 4)
        out[b, T * q:T * (q + 1), :] = res.results[c]["out"]
    if _DEBUG_TAPS:
        kernel.last_results = res.results
    return out
